# version 23
# baseline (speedup 1.0000x reference)
import numpy as np
from contextlib import ExitStack

import concourse.bass as bass
import concourse.mybir as mybir
from concourse.bass_utils import run_bass_kernel_spmd

F32 = mybir.dt.float32
BF16 = mybir.dt.bfloat16
AF = mybir.ActivationFunctionType
ALU = mybir.AluOpType
AX = mybir.AxisListType.X

S = 2048
DM = 2048
T = 16
EPS = 1e-5
INV_N = 1.0 / DM

A_ACT = 6
A_DVE = 2
B_ACT = 9
B_DVE = 14
B_PE = 4
B_POOL = 3


def build_nc():
    nc = bass.Bass("TRN2", target_bir_lowering=False)

    d_x = nc.dram_tensor("x", [S, DM], F32, kind="ExternalInput").ap()
    d_wg = nc.dram_tensor("wg_t", [128, 128, 2048], F32, kind="ExternalInput").ap()
    d_wd = nc.dram_tensor("wd_t", [4, 64, 128, 512], F32, kind="ExternalInput").ap()
    d_win = nc.dram_tensor("win_t", [128, 128], F32, kind="ExternalInput").ap()
    d_wdn = nc.dram_tensor("wdn_t", [128, 512], F32, kind="ExternalInput").ap()
    d_wout = nc.dram_tensor("wout", [8, DM], F32, kind="ExternalInput").ap()
    d_wup = nc.dram_tensor("wup_aug", [33, DM], F32, kind="ExternalInput").ap()
    d_gam = nc.dram_tensor("gamma_t", [128, 16], F32, kind="ExternalInput").ap()
    d_bet = nc.dram_tensor("beta_t", [128, 16], F32, kind="ExternalInput").ap()
    d_gamr = nc.dram_tensor("gamma_rep", [128, DM], F32, kind="ExternalInput").ap()
    d_dskr = nc.dram_tensor("dsk_rep", [128, DM], F32, kind="ExternalInput").ap()
    d_boutr = nc.dram_tensor("bout_rep", [128, DM], F32, kind="ExternalInput").ap()
    d_betr = nc.dram_tensor("beta_rep", [128, DM], F32, kind="ExternalInput").ap()
    d_bin = nc.dram_tensor("b_in", [8, 1], F32, kind="ExternalInput").ap()
    d_bdn = nc.dram_tensor("b_down", [32, 1], F32, kind="ExternalInput").ap()
    d_dlg = nc.dram_tensor("dlogit", [8, 1], F32, kind="ExternalInput").ap()
    d_id = nc.dram_tensor("ident", [128, 128], F32, kind="ExternalInput").ap()

    d_x2 = nc.dram_tensor("x2_scr", [S, DM], F32, kind="Internal").ap()
    d_x2bf = nc.dram_tensor("x2bf_scr", [S, DM], BF16, kind="Internal").ap()
    d_s4 = nc.dram_tensor("s4_scr", [S, DM], F32, kind="Internal").ap()
    d_out = nc.dram_tensor("out", [S, DM], F32, kind="ExternalOutput").ap()

    es = ExitStack()
    with es:
        ec = es.enter_context
        PSA = ec(nc.psum_tensor("PSA", [128, 2048], F32))
        PSB = ec(nc.psum_tensor("PSB", [128, 1024], F32))
        PSC = ec(nc.psum_tensor("PSC", [128, 1024], F32))

        names = (["p0", "p0s", "p0d",
                  "ax0", "ax1", "adve", "aact", "ape",
                  "bx0", "bx1", "bdve", "bact", "bpe", "bpool", "brep",
                  "x2w0", "x2w1", "x2bfw0", "x2bfw1",
                  "ctp", "cpe", "cact", "cdve",
                  "cwg0", "cwg1", "cwg2", "cwg3",
                  "dpe", "ddve", "dact", "x2r",
                  "s4w0", "s4w1", "s4r0", "s4r1", "outw0", "outw1", "vsync",
                  "vv", "pv", "av", "xnw0", "xnw1"]
                 + ["dwd%d" % i for i in range(8)])
        sems = {n: ec(nc.semaphore(n)) for n in names}

        def SEM(n):
            return sems[n]

        _gc = {"vv": [0], "pv": [0], "av": [0]}

        def GRD(eng, name, ins):
            c = _gc[name]
            c[0] += 1
            ins.then_inc(SEM(name), 1)
            eng.wait_ge(SEM(name), c[0])

        eps_sb = ec(nc.sbuf_tensor("eps_sb", [128, 1], F32))
        x2T = ec(nc.sbuf_tensor("x2T", [128, 16 * 1024], BF16))

        ab_es = ExitStack()
        with ab_es:
            ab = ab_es.enter_context
            ident = ab(nc.sbuf_tensor("ident_sb", [128, 128], F32))
            wtil = ab(nc.sbuf_tensor("wtil_bf", [128, 128], BF16))
            wdnt = ab(nc.sbuf_tensor("wdnt_bf", [128, 512], BF16))
            wout_bf = ab(nc.sbuf_tensor("wout_bf", [8, DM], BF16))
            wup_bf = ab(nc.sbuf_tensor("wup_bf", [33, DM], BF16))
            c2 = ab(nc.sbuf_tensor("c2", [8, 1], F32))
            c2d = ab(nc.sbuf_tensor("c2d", [32, 1], F32))
            hbf = ab(nc.sbuf_tensor("hbf", [8, S], BF16))
            hfT = ab(nc.sbuf_tensor("hfT_aug", [33, 128], BF16))
            ag1 = ab(nc.sbuf_tensor("ag1", [128, 2 * T], F32))
            bnst = ab(nc.sbuf_tensor("bnst", [128, 24], F32))
            rsds1 = ab(nc.sbuf_tensor("rsds1", [128, T], F32))
            nmrs1 = ab(nc.sbuf_tensor("nmrs1", [128, T], F32))
            tiny = ab(nc.sbuf_tensor("tiny", [128, 8], F32))
            actA = ab(nc.sbuf_tensor("actA", [128, 2], F32))

            p0_es = ExitStack()
            with p0_es, nc.named_scope("P0"):
                pc = p0_es.enter_context
                win_f = pc(nc.sbuf_tensor("win_f", [128, 128], F32))
                wdn_f = pc(nc.sbuf_tensor("wdn_f", [128, 512], F32))
                win_bf = pc(nc.sbuf_tensor("win_bf", [128, 128], BF16))
                wdn_bf = pc(nc.sbuf_tensor("wdn_bf", [128, 512], BF16))
                gam_t = pc(nc.sbuf_tensor("gam_t", [128, 16], F32))
                bet_t = pc(nc.sbuf_tensor("bet_t", [128, 16], F32))
                bet_bf = pc(nc.sbuf_tensor("bet_bf", [128, 16], BF16))
                wout_f = pc(nc.sbuf_tensor("wout_f", [8, DM], F32))
                wup_f = pc(nc.sbuf_tensor("wup_f", [33, DM], F32))
                sm8 = pc(nc.sbuf_tensor("sm8", [32, 4], F32))
                blk = pc(nc.Block())

                @blk.sync
                def _(sy):
                    for ap_d, ap_s in [(ident[:], d_id), (win_f[:], d_win),
                                       (wdn_f[:], d_wdn), (gam_t[:], d_gam),
                                       (bet_t[:], d_bet), (wout_f[:], d_wout),
                                       (wup_f[:], d_wup), (sm8[0:8, 0:1], d_bin),
                                       (sm8[0:32, 1:2], d_bdn), (sm8[0:8, 2:3], d_dlg)]:
                        sy.dma_start(out=ap_d, in_=ap_s).then_inc(SEM("p0d"), 16)

                @blk.vector
                def _(v):
                    v.wait_ge(SEM("p0d"), 160)
                    v.tensor_copy(bet_bf[:], bet_t[:])
                    v.tensor_copy(win_bf[:], win_f[:])
                    v.tensor_copy(wdn_bf[:], wdn_f[:])
                    v.tensor_copy(wout_bf[:], wout_f[:])
                    v.tensor_copy(wup_bf[:], wup_f[:])
                    w3i = win_f[:].rearrange("p (j c) -> p j c", c=8)
                    w3o = wtil[:].rearrange("p (j c) -> p j c", c=8)
                    for c in range(8):
                        v.tensor_tensor(w3o[:, :, c], w3i[:, :, c], gam_t[:], ALU.mult)
                    d3i = wdn_f[:].rearrange("p (j c) -> p j c", c=32)
                    d3o = wdnt[:].rearrange("p (j c) -> p j c", c=32)
                    for c in range(32):
                        v.tensor_tensor(d3o[:, :, c], d3i[:, :, c], gam_t[:], ALU.mult)
                    v.memset(eps_sb[:], EPS)
                    v.memset(hfT[:], 1.0).then_inc(SEM("p0"), 1)
                    v.wait_ge(SEM("p0s"), 1)
                    v.nop().then_inc(SEM("p0"), 1)
                    v.wait_ge(SEM("p0"), 3)
                    v.tensor_tensor(c2[:], PSB[0:8, 0:1], sm8[0:8, 0:1], ALU.add) \
                        .then_inc(SEM("p0"), 1)
                    v.wait_ge(SEM("p0"), 5)
                    v.tensor_tensor(c2d[:], PSB[0:32, 0:1], sm8[0:32, 1:2], ALU.add) \
                        .then_inc(SEM("p0"), 1)

                @blk.scalar
                def _(sc):
                    sc.wait_ge(SEM("p0d"), 160)
                    sc.nop().then_inc(SEM("p0s"), 1)

                @blk.tensor
                def _(te):
                    te.wait_ge(SEM("p0"), 1)
                    for j in range(16):
                        ins = te.matmul(PSB[0:8, 0:1], win_bf[:, j * 8:(j + 1) * 8],
                                        bet_bf[:, j:j + 1],
                                        start=(j == 0), stop=(j == 15))
                        if j == 15:
                            ins.then_inc(SEM("p0"), 1)
                    te.wait_ge(SEM("p0"), 4)
                    for j in range(16):
                        ins = te.matmul(PSB[0:32, 0:1], wdn_bf[:, j * 32:(j + 1) * 32],
                                        bet_bf[:, j:j + 1],
                                        start=(j == 0), stop=(j == 15))
                        if j == 15:
                            ins.then_inc(SEM("p0"), 1)

            a_es = ExitStack()
            with a_es, nc.named_scope("PA"):
                ac_ = a_es.enter_context
                xa0 = ac_(nc.sbuf_tensor("xbA0", [128, DM], F32))
                xa1 = ac_(nc.sbuf_tensor("xbA1", [128, DM], F32))
                xna0 = ac_(nc.sbuf_tensor("xnpA0", [128, DM], BF16))
                xna1 = ac_(nc.sbuf_tensor("xnpA1", [128, DM], BF16))
                xnTF = ac_(nc.sbuf_tensor("xnT_full", [128, 16 * DM], BF16))
                u_sb = ac_(nc.sbuf_tensor("u_sb", [8, S], F32))
                h_sb = ac_(nc.sbuf_tensor("h_sb", [8, S], F32))
                afull = ac_(nc.sbuf_tensor("afull", [8, S], F32))
                blk = ac_(nc.Block())
                xb = [xa0, xa1]
                xnp = [xna0, xna1]

                @blk.sync
                def _(sy):
                    for i in range(T + 1):
                        if i < T:
                            if i >= 2:
                                sy.wait_ge(SEM("aact"), 4 * (i - 2) + 4)
                            sy.dma_start(out=xb[i % 2][:],
                                         in_=d_x[i * 128:(i + 1) * 128, :]) \
                                .then_inc(SEM("ax%d" % (i % 2)), 16)
                        if i >= 1:
                            j = i - 1
                            sy.wait_ge(SEM("aact"), 4 * j + 4)
                            sy.dma_start(out=d_x2bf[j * 128:(j + 1) * 128, :],
                                         in_=xnp[j % 2][:]) \
                                .then_inc(SEM("xnw%d" % (j % 2)), 16)


                @blk.scalar
                def _(sc):
                    sc.dma_start(out=actA[0:8, 0:1], in_=d_dlg).then_inc(SEM("p0d"), 16)
                    sc.wait_ge(SEM("p0d"), 176)
                    sc.activation(actA[0:8, 1:2], actA[0:8, 0:1], AF.Sigmoid) \
                        .then_inc(SEM("p0s"), 1)
                    for i in range(T):
                        par = i % 2
                        sc.wait_ge(SEM("adve"), 1 if i == 0 else 2 * i)
                        sc.activation(actA[:, 0:1], ag1[:, 2 * i + 1:2 * i + 2],
                                      AF.Ln, bias=eps_sb[:, 0:1]) \
                            .then_inc(SEM("aact"), 1)
                        sc.wait_ge(SEM("aact"), 4 * i + 1)
                        sc.activation(rsds1[:, i:i + 1], actA[:, 0:1], AF.Exp,
                                      scale=-0.5).then_inc(SEM("aact"), 2)
                        sc.wait_ge(SEM("adve"), min(2 * i + 3, 2 * T))
                        if i >= 2:
                            sc.wait_ge(SEM("xnw%d" % (i % 2)), 16 * (i // 2))
                        sc.activation(xnp[par][:], xb[par][:], AF.Identity,
                                      bias=nmrs1[:, i:i + 1],
                                      scale=rsds1[:, i:i + 1]).then_inc(SEM("aact"), 1)
                    sc.wait_ge(SEM("xnw0"), 16 * 8)
                    sc.wait_ge(SEM("xnw1"), 16 * 8)
                    for j in range(16):
                        sc.dma_start(out=xnTF[:, j * DM:(j + 1) * DM],
                                     in_=d_x2bf[0:S, j * 128:(j + 1) * 128],
                                     transpose=True).then_inc(SEM("ctp"), 16)
                    for tc in range(4):
                        sc.wait_ge(SEM("ape"), tc + 1)
                        sc.activation(u_sb[0:8, tc * 512:(tc + 1) * 512],
                                      PSB[0:8, (tc % 2) * 512:(tc % 2) * 512 + 512],
                                      AF.Identity, bias=c2[:, 0:1]) \
                            .then_inc(SEM("aact"), 1)

                @blk.vector
                def _(v):
                    v.wait_ge(SEM("p0s"), 2)
                    GRD(v, "vv", v.memset(afull[:], 1.0))
                    v.tensor_scalar_mul(afull[:], afull[:], actA[0:8, 1:2])
                    def AGGR_V(i):
                        return 1 if i == 0 else 2 * i

                    def NMR_V(i):
                        return 2 * i + 3

                    for i in range(T + 1):
                        if i < T:
                            par = i % 2
                            v.wait_ge(SEM("ax%d" % par), 16 * (i // 2 + 1))
                            X = xb[par][:]
                            for k in range(4):
                                ins = v.bn_stats(bnst[:, k * 6:(k + 1) * 6],
                                                 X[:, k * 512:(k + 1) * 512])
                            GRD(v, "vv", ins)
                            v.bn_aggr(ag1[:, 2 * i:2 * i + 2], bnst[:]) \
                                .then_inc(SEM("adve"), 1)
                        if i >= 1:
                            j = i - 1
                            v.wait_ge(SEM("aact"), 4 * j + 3)
                            v.scalar_tensor_tensor(nmrs1[:, j:j + 1],
                                                   ag1[:, 2 * j:2 * j + 1],
                                                   -1.0, rsds1[:, j:j + 1],
                                                   ALU.mult, ALU.mult) \
                                .then_inc(SEM("adve"), 1)
                    v.wait_ge(SEM("aact"), 4 * T + 4)
                    GRD(v, "vv", v.tensor_tensor_scan(h_sb[:], afull[:], u_sb[:], 0.0,
                                                      ALU.mult, ALU.add))
                    v.tensor_copy(hbf[:], h_sb[:]).then_inc(SEM("adve"), 1)

                @blk.tensor
                def _(te):
                    te.wait_ge(SEM("ctp"), 256)
                    for tc in range(4):
                        if tc >= 2:
                            te.wait_ge(SEM("aact"), 4 * T + (tc - 2) + 1)
                        for j in range(16):
                            ins = te.matmul(
                                PSB[0:8, (tc % 2) * 512:(tc % 2) * 512 + 512],
                                wtil[:, j * 8:(j + 1) * 8],
                                xnTF[:, j * DM + tc * 512:j * DM + (tc + 1) * 512],
                                start=(j == 0), stop=(j == 15))
                            if j == 15:
                                ins.then_inc(SEM("ape"), 1)

            b_es = ExitStack()
            with b_es, nc.named_scope("PB"):
                bc = b_es.enter_context
                xbb0 = bc(nc.sbuf_tensor("xbB0", [128, DM], F32))
                xbb1 = bc(nc.sbuf_tensor("xbB1", [128, DM], F32))
                xnb0 = bc(nc.sbuf_tensor("xnpB0", [128, DM], F32))
                xnb1 = bc(nc.sbuf_tensor("xnpB1", [128, DM], F32))
                sp0 = bc(nc.sbuf_tensor("scrp0", [128, DM], F32))
                sp1 = bc(nc.sbuf_tensor("scrp1", [128, DM], F32))
                x10 = bc(nc.sbuf_tensor("x1B0", [128, DM], F32))
                x11 = bc(nc.sbuf_tensor("x1B1", [128, DM], F32))
                q20 = bc(nc.sbuf_tensor("xn2p0", [128, DM], F32))
                q21 = bc(nc.sbuf_tensor("xn2p1", [128, DM], F32))
                qt0 = bc(nc.sbuf_tensor("xn2T0", [128, DM], BF16))
                qt1 = bc(nc.sbuf_tensor("xn2T1", [128, DM], BF16))
                sf0 = bc(nc.sbuf_tensor("scrf0", [128, DM], F32))
                sf1 = bc(nc.sbuf_tensor("scrf1", [128, DM], F32))
                ss0 = bc(nc.sbuf_tensor("sB0", [128, DM], F32))
                ss1 = bc(nc.sbuf_tensor("sB1", [128, DM], F32))
                x20 = bc(nc.sbuf_tensor("x2B0", [128, DM], F32))
                x21 = bc(nc.sbuf_tensor("x2B1", [128, DM], F32))
                xf0 = bc(nc.sbuf_tensor("x2bf0", [128, DM], BF16))
                xf1 = bc(nc.sbuf_tensor("x2bf1", [128, DM], BF16))
                gd_rep = bc(nc.sbuf_tensor("gd_rep", [128, DM], F32))
                ball_rep = bc(nc.sbuf_tensor("ball_rep", [128, DM], F32))
                ag2 = bc(nc.sbuf_tensor("ag2", [128, 2 * T], F32))
                ag3 = bc(nc.sbuf_tensor("ag3", [128, 2 * T], F32))
                rn2 = bc(nc.sbuf_tensor("rn2", [128, 2 * T], F32))
                rn3 = bc(nc.sbuf_tensor("rn3", [128, 2 * T], F32))
                actB = bc(nc.sbuf_tensor("actB", [128, 2], F32))
                blk = bc(nc.Block())
                bxb = [xbb0, xbb1]
                bxnp = [xnb0, xnb1]
                scrp = [sp0, sp1]
                x1b = [x10, x11]
                xn2p = [q20, q21]
                xn2T = [qt0, qt1]
                scrf = [sf0, sf1]
                s_b = [ss0, ss1]
                x2b = [x20, x21]
                x2bf = [xf0, xf1]

                @blk.sync
                def _(sy):
                    sy.dma_start(out=gd_rep[:], in_=d_gamr).then_inc(SEM("brep"), 16)
                    sy.dma_start(out=sp0[:], in_=d_dskr).then_inc(SEM("brep"), 16)
                    sy.dma_start(out=sp1[:], in_=d_boutr).then_inc(SEM("brep"), 16)
                    sy.dma_start(out=ball_rep[:], in_=d_betr).then_inc(SEM("brep"), 16)
                    for i in range(T + 2):
                        if i < T:
                            if i >= 2:
                                sy.wait_ge(SEM("bpool"), B_POOL * (i - 2) + 3)
                            sy.dma_start(out=bxb[i % 2][:],
                                         in_=d_x[i * 128:(i + 1) * 128, :]) \
                                .then_inc(SEM("bx%d" % (i % 2)), 16)
                        if i >= 2:
                            j = i - 2
                            sy.wait_ge(SEM("bact"), 9 * j + 10)
                            sy.dma_start(out=d_x2[j * 128:(j + 1) * 128, :],
                                         in_=x2b[j % 2][:]) \
                                .then_inc(SEM("x2w%d" % (j % 2)), 16)

                @blk.gpsimd
                def _(g):
                    g.wait_ge(SEM("brep"), 64)
                    GRD(g, "pv", g.tensor_tensor(gd_rep[:], gd_rep[:],
                                                 sp0[:], ALU.mult))
                    GRD(g, "pv", g.tensor_tensor(ball_rep[:], ball_rep[:],
                                                 sp0[:], ALU.mult))
                    g.tensor_tensor(ball_rep[:], ball_rep[:], sp1[:], ALU.add) \
                        .then_inc(SEM("brep"), 1)
                    for i in range(T):
                        par = i % 2
                        g.wait_ge(SEM("bact"), 1 if i == 0 else 9 * i - 2)
                        if i >= 2:
                            g.wait_ge(SEM("bdve"), B_DVE * (i - 2) + 2)
                        GRD(g, "pv", g.tensor_tensor(scrp[par][:], bxnp[par][:],
                                                      gd_rep[:], ALU.mult))
                        g.tensor_tensor(scrp[par][:], scrp[par][:], ball_rep[:],
                                        ALU.add).then_inc(SEM("bpool"), 2)
                        g.wait_ge(SEM("bdve"), B_DVE * i + 2)
                        if i >= 2:
                            g.wait_ge(SEM("bdve"), B_DVE * (i - 2) + 8)
                        g.tensor_tensor(scrf[par][:], bxb[par][:], x1b[par][:],
                                        ALU.add).then_inc(SEM("bpool"), 1)

                @blk.vector
                def _(v):
                    for i in range(T):
                        par = i % 2
                        X = bxb[par][:]
                        v.wait_ge(SEM("bpe"), B_PE * i + 1)
                        v.wait_ge(SEM("bx%d" % par), 16 * (i // 2 + 1))
                        if i >= 2:
                            v.wait_ge(SEM("bact"), 9 * (i - 2) + 4)
                            v.wait_ge(SEM("bpool"), B_POOL * (i - 2) + 3)
                        v.scalar_tensor_tensor(x1b[par][:], PSA[:], 1.0, X,
                                               ALU.mult, ALU.add) \
                            .then_inc(SEM("bdve"), 1)
                        v.wait_ge(SEM("bdve"), B_DVE * i + 1)
                        v.wait_ge(SEM("bpool"), B_POOL * i + 2)
                        v.tensor_tensor(x1b[par][:], x1b[par][:],
                                        scrp[par][:], ALU.add) \
                            .then_inc(SEM("bdve"), 1)
                        v.wait_ge(SEM("bdve"), B_DVE * i + 2)
                        v.wait_ge(SEM("bdve"), B_DVE * i + 2)
                        for k in range(4):
                            ins = v.bn_stats(bnst[:, k * 6:(k + 1) * 6],
                                             x1b[par][:, k * 512:(k + 1) * 512])
                        GRD(v, "vv", ins)
                        v.bn_aggr(ag2[:, 2 * i:2 * i + 2], bnst[:]) \
                            .then_inc(SEM("bdve"), 4)
                        v.wait_ge(SEM("bact"), 9 * i + 3)
                        v.scalar_tensor_tensor(rn2[:, T + i:T + i + 1],
                                               ag2[:, 2 * i:2 * i + 1], -1.0,
                                               rn2[:, i:i + 1], ALU.mult, ALU.mult) \
                            .then_inc(SEM("bdve"), 1)
                        v.wait_ge(SEM("bpe"), B_PE * i + 4)
                        v.wait_ge(SEM("bpool"), B_POOL * i + 3)
                        if i >= 2:
                            v.wait_ge(SEM("bact"), 9 * (i - 2) + 10)
                        v.scalar_tensor_tensor(s_b[par][:, 0:1024], PSB[:], 1.0,
                                               scrf[par][:, 0:1024],
                                               ALU.mult, ALU.add)
                        v.scalar_tensor_tensor(s_b[par][:, 1024:2048], PSC[:], 1.0,
                                               scrf[par][:, 1024:2048],
                                               ALU.mult, ALU.add) \
                            .then_inc(SEM("bdve"), 1)
                        v.wait_ge(SEM("bdve"), B_DVE * i + 8)
                        for k in range(4):
                            ins = v.bn_stats(bnst[:, k * 6:(k + 1) * 6],
                                             s_b[par][:, k * 512:(k + 1) * 512])
                        GRD(v, "vv", ins)
                        v.bn_aggr(ag3[:, 2 * i:2 * i + 2], bnst[:]) \
                            .then_inc(SEM("bdve"), 4)
                        v.wait_ge(SEM("bact"), 9 * i + 9)
                        v.scalar_tensor_tensor(rn3[:, T + i:T + i + 1],
                                               ag3[:, 2 * i:2 * i + 1], -1.0,
                                               rn3[:, i:i + 1], ALU.mult, ALU.mult) \
                            .then_inc(SEM("bdve"), 1)
                        v.wait_ge(SEM("bact"), 9 * i + 10)
                        if i >= 2:
                            v.wait_ge(SEM("x2bfw%d" % par), 16 * (i // 2))
                        v.tensor_copy(x2bf[par][:], x2b[par][:]) \
                            .then_inc(SEM("bdve"), 1)

                @blk.scalar
                def _(sc):
                    sc.wait_ge(SEM("bx0"), 16)
                    sc.wait_ge(SEM("brep"), 65)
                    sc.activation(bxnp[0][:], bxb[0][:], AF.Identity,
                                  bias=nmrs1[:, 0:1],
                                  scale=rsds1[:, 0:1]).then_inc(SEM("bact"), 1)
                    for i in range(T):
                        par = i % 2
                        sc.wait_ge(SEM("bdve"), B_DVE * i + 6)
                        sc.activation(actB[:, 0:1], ag2[:, 2 * i + 1:2 * i + 2],
                                      AF.Ln, bias=eps_sb[:, 0:1]) \
                            .then_inc(SEM("bact"), 1)
                        sc.wait_ge(SEM("bact"), 9 * i + 2)
                        sc.activation(rn2[:, i:i + 1], actB[:, 0:1], AF.Exp,
                                      scale=-0.5).then_inc(SEM("bact"), 1)
                        sc.wait_ge(SEM("bdve"), B_DVE * i + 7)
                        if i >= 2:
                            sc.wait_ge(SEM("bpe"), B_PE * (i - 2) + 2)
                        sc.activation(xn2p[par][:], x1b[par][:], AF.Identity,
                                      bias=rn2[:, T + i:T + i + 1],
                                      scale=rn2[:, i:i + 1]).then_inc(SEM("bact"), 1)
                        sc.wait_ge(SEM("bpe"), B_PE * i + 2)
                        sc.activation(xn2T[par][:], PSA[:], AF.Copy) \
                            .then_inc(SEM("bact"), 1)
                        sc.wait_ge(SEM("bpe"), B_PE * i + 3)
                        sc.activation(hfT[0:32, :], PSA[0:32, 0:128], AF.Relu,
                                      bias=c2d[:, 0:1])
                        sc.drain().then_inc(SEM("bact"), 1)
                        if i < T - 1:
                            np_ = (i + 1) % 2
                            sc.wait_ge(SEM("bx%d" % np_), 16 * ((i + 1) // 2 + 1))
                            if i >= 1:
                                sc.wait_ge(SEM("bpool"), B_POOL * (i - 1) + 1)
                            sc.activation(bxnp[np_][:], bxb[np_][:], AF.Identity,
                                          bias=nmrs1[:, i + 1:i + 2],
                                          scale=rsds1[:, i + 1:i + 2]) \
                                .then_inc(SEM("bact"), 1)
                        else:
                            sc.nop().then_inc(SEM("bact"), 1)
                        sc.wait_ge(SEM("bdve"), B_DVE * i + 12)
                        sc.activation(actB[:, 1:2], ag3[:, 2 * i + 1:2 * i + 2],
                                      AF.Ln, bias=eps_sb[:, 0:1]) \
                            .then_inc(SEM("bact"), 1)
                        sc.wait_ge(SEM("bact"), 9 * i + 8)
                        sc.activation(rn3[:, i:i + 1], actB[:, 1:2], AF.Exp,
                                      scale=-0.5).then_inc(SEM("bact"), 1)
                        sc.wait_ge(SEM("bdve"), B_DVE * i + 13)
                        if i >= 2:
                            sc.wait_ge(SEM("x2w%d" % par), 16 * (i // 2))
                            sc.wait_ge(SEM("bdve"), B_DVE * (i - 2) + 14)
                        sc.activation(x2b[par][:], s_b[par][:], AF.Identity,
                                      bias=rn3[:, T + i:T + i + 1],
                                      scale=rn3[:, i:i + 1]).then_inc(SEM("bact"), 1)
                        sc.wait_ge(SEM("bdve"), B_DVE * i + 14)
                        sc.dma_start(out=d_x2bf[i * 128:(i + 1) * 128, :],
                                     in_=x2bf[par][:]) \
                            .then_inc(SEM("x2bfw%d" % par), 16)
                        if i == 7:
                            sc.wait_ge(SEM("x2bfw0"), 16 * 4)
                            sc.wait_ge(SEM("x2bfw1"), 16 * 4)
                            for j in range(16):
                                sc.dma_start(out=x2T[:, j * 1024:(j + 1) * 1024],
                                             in_=d_x2bf[0:1024,
                                                        j * 128:(j + 1) * 128],
                                             transpose=True).then_inc(SEM("ctp"), 16)

                @blk.tensor
                def _(te):
                    for i in range(T):
                        par = i % 2
                        if i >= 1:
                            te.wait_ge(SEM("bact"), 9 * (i - 1) + 6)
                            te.wait_ge(SEM("bdve"), B_DVE * (i - 1) + 1)
                        for cc in range(4):
                            ins = te.matmul(PSA[:, cc * 512:(cc + 1) * 512],
                                            hbf[0:8, i * 128:(i + 1) * 128],
                                            wout_bf[0:8, cc * 512:(cc + 1) * 512],
                                            start=True, stop=True)
                            if cc == 3:
                                ins.then_inc(SEM("bpe"), 1)
                        te.wait_ge(SEM("bact"), 9 * i + 4)
                        te.wait_ge(SEM("bdve"), B_DVE * i + 1)
                        for k in range(16):
                            ins = te.transpose(PSA[:, k * 128:(k + 1) * 128],
                                               xn2p[par][:, k * 128:(k + 1) * 128],
                                               ident[:])
                            if k == 15:
                                ins.then_inc(SEM("bpe"), 1)
                        te.wait_ge(SEM("bact"), 9 * i + 5)
                        for j in range(16):
                            ins = te.matmul(PSA[0:32, 0:128],
                                            wdnt[:, j * 32:(j + 1) * 32],
                                            xn2T[par][:, j * 128:(j + 1) * 128],
                                            start=(j == 0), stop=(j == 15))
                            if j == 15:
                                ins.then_inc(SEM("bpe"), 1)
                        te.wait_ge(SEM("bact"), 9 * i + 6)
                        if i >= 1:
                            te.wait_ge(SEM("bdve"), B_DVE * (i - 1) + 8)
                        for cc in range(4):
                            dst = PSB[:, (cc % 2) * 512:(cc % 2 + 1) * 512] if cc < 2 \
                                else PSC[:, (cc % 2) * 512:(cc % 2 + 1) * 512]
                            ins = te.matmul(dst,
                                            hfT[0:33, 0:128],
                                            wup_bf[0:33, cc * 512:(cc + 1) * 512],
                                            start=True, stop=True)
                            if cc == 3:
                                ins.then_inc(SEM("bpe"), 1)

        with nc.sbuf_tensor("hhT", [128, 64 * 1024], BF16) as hhT, \
             nc.sbuf_tensor("wdr", [128, 2 * 8 * 512], BF16) as wdr:
            for H in range(2):
                t0 = H * 1024
                CPE0 = 64 * H
                CA0 = 128 * H
                CD0 = 128 * H

                c_es = ExitStack()
                with c_es, nc.named_scope("PC%d" % H):
                    cc_ = c_es.enter_context
                    wgb0 = cc_(nc.sbuf_tensor("wgb0_%d" % H, [128, 2048], BF16))
                    wgb1 = cc_(nc.sbuf_tensor("wgb1_%d" % H, [128, 2048], BF16))
                    wgb2 = cc_(nc.sbuf_tensor("wgb2_%d" % H, [128, 2048], BF16))
                    wgb3 = cc_(nc.sbuf_tensor("wgb3_%d" % H, [128, 2048], BF16))
                    sil0 = cc_(nc.sbuf_tensor("sil0_%d" % H, [128, 1024], F32))
                    sil1 = cc_(nc.sbuf_tensor("sil1_%d" % H, [128, 1024], F32))
                    blk = cc_(nc.Block())
                    wgb = [wgb0, wgb1, wgb2, wgb3]
                    sil = [sil0, sil1]

                    @blk.scalar
                    def _(sc):
                        for gp in range(64):
                            sc.wait_ge(SEM("cpe"), CPE0 + gp + 1)
                            if gp >= 2:
                                sc.wait_ge(SEM("cdve"), CD0 + 2 * (gp - 2) + 2)
                            pi = gp % 2
                            for c in range(2):
                                sc.activation(sil[pi][:, c * 512:(c + 1) * 512],
                                              PSA[:, pi * 1024 + c * 512:
                                                  pi * 1024 + (c + 1) * 512],
                                              AF.Silu).then_inc(SEM("cact"), 1)

                    @blk.gpsimd
                    def _(g):
                        for idx in range(128):
                            gp, role = idx // 2, idx % 2
                            ot = gp + 64 * role
                            if idx >= 4:
                                g.wait_ge(SEM("cpe"), CPE0 + (idx - 4) // 2 + 1)
                            g.dma_start(out=wgb[idx % 4][:], in_=d_wg[ot]) \
                                .then_inc(SEM("cwg%d" % (idx % 4)), 16)
                        if H == 1:
                            g.wait_ge(SEM("dpe"), 32)
                        for pidx in range(2):
                            g.dma_start(
                                out=wdr[:, pidx * 4096:(pidx + 1) * 4096],
                                in_=d_wd[0, pidx * 8:(pidx + 1) * 8]
                                .rearrange("g p c -> p g c")) \
                                .then_inc(SEM("dwd%d" % pidx, ), 16)

                    @blk.tensor
                    def _(te):
                        te.wait_ge(SEM("ctp"), 256 * (H + 2))
                        for gp in range(64):
                            pi = gp % 2
                            if gp >= 2:
                                te.wait_ge(SEM("cact"), CA0 + 2 * (gp - 2) + 2)
                                te.wait_ge(SEM("cdve"), CD0 + 2 * (gp - 2) + 2)
                            for role in range(2):
                                idx = 2 * gp + role
                                te.wait_ge(SEM("cwg%d" % (idx % 4)),
                                           16 * (32 * H + idx // 4 + 1))
                                dst = PSA[:, pi * 1024:(pi + 1) * 1024] if role == 0 \
                                    else (PSB[:] if pi == 0 else PSC[:])
                                for c in range(2):
                                    for k in range(16):
                                        ins = te.matmul(
                                            dst[:, c * 512:(c + 1) * 512],
                                            wgb[idx % 4][:, k * 128:(k + 1) * 128],
                                            x2T[:, k * 1024 + c * 512:
                                                k * 1024 + (c + 1) * 512],
                                            start=(k == 0), stop=(k == 15))
                                        if role == 1 and c == 1 and k == 15:
                                            ins.then_inc(SEM("cpe"), 1)

                    @blk.vector
                    def _(v):
                        for gp in range(64):
                            pi = gp % 2
                            ups = PSB[:] if pi == 0 else PSC[:]
                            v.wait_ge(SEM("cpe"), CPE0 + gp + 1)
                            for c in range(2):
                                v.wait_ge(SEM("cact"), CA0 + 2 * gp + c + 1)
                                v.tensor_tensor(
                                    hhT[:, gp * 1024 + c * 512:gp * 1024 + (c + 1) * 512],
                                    sil[pi][:, c * 512:(c + 1) * 512],
                                    ups[:, c * 512:(c + 1) * 512], ALU.mult) \
                                    .then_inc(SEM("cdve"), 1)

                WD0 = 256 * H
                PD0 = 256 * H
                PD0X = 32 * H
                DD0 = 80 * H
                X2R0 = 32 * H
                DA0 = 16 * H

                d_es = ExitStack()
                with d_es, nc.named_scope("PD%d" % H):
                    dc = d_es.enter_context
                    evb8 = dc(nc.sbuf_tensor("evb8_%d" % H, [128, 8 * 512], F32))

                    pst = dc(nc.sbuf_tensor("pst_%d" % H, [128, 192], F32))
                    ag4 = dc(nc.sbuf_tensor("ag4_%d" % H, [128, 16], F32))
                    acD = dc(nc.sbuf_tensor("acD_%d" % H, [128, 2], F32))
                    rn4 = dc(nc.sbuf_tensor("rn4_%d" % H, [128, 16], F32))
                    blk = dc(nc.Block())
                    def evb(tl):
                        return evb8[:, tl * 512:(tl + 1) * 512]
                    finv = evb8[:, 0:2048]
                    obv = evb8[:, 2048:4096]
                    fin = [finv, finv]
                    ob = [obv, obv]

                    def pstl(tl):
                        if tl < 4:
                            return PSA[:, tl * 512:(tl + 1) * 512]
                        if tl < 6:
                            return PSB[:, (tl - 4) * 512:(tl - 3) * 512]
                        return PSC[:, (tl - 6) * 512:(tl - 5) * 512]

                    @blk.gpsimd
                    def _(g):
                        for hs in range(4):
                            for b in range(8):
                                idx = hs * 8 + b
                                if idx < 2:
                                    continue
                                g.wait_ge(SEM("dpe"), PD0X + idx - 1)
                                g.dma_start(
                                    out=wdr[:, (idx % 2) * 4096:(idx % 2 + 1) * 4096],
                                    in_=d_wd[hs, b * 8:(b + 1) * 8]
                                    .rearrange("g p c -> p g c")) \
                                    .then_inc(SEM("dwd%d" % (idx % 2)), 16)

                    @blk.sync
                    def _(sy):
                        if H == 0:
                            sy.wait_ge(SEM("x2w0"), 16 * (T // 2))
                            sy.wait_ge(SEM("x2w1"), 16 * (T // 2))
                        for hs in range(4):
                            if hs >= 1:
                                sy.wait_ge(SEM("ddve"), DD0 + 16 * hs)
                            if hs >= 1:
                                sy.wait_ge(SEM("s4w0"), 16 * (32 * H + 8 * hs))
                            for tl in range(8):
                                sy.dma_start(
                                    out=evb(tl),
                                    in_=d_x2[t0 + tl * 128:t0 + (tl + 1) * 128,
                                             hs * 512:(hs + 1) * 512]) \
                                    .then_inc(SEM("x2r"), 16)
                            for tl in range(8):
                                sy.wait_ge(SEM("ddve"), DD0 + 16 * hs + tl + 1)
                                sy.dma_start(
                                    out=d_s4[t0 + tl * 128:t0 + (tl + 1) * 128,
                                             hs * 512:(hs + 1) * 512],
                                    in_=evb(tl)) \
                                    .then_inc(SEM("s4w0"), 16)
                        sy.wait_ge(SEM("s4w0"), 16 * 32 * (H + 1))
                        sy.wait_ge(SEM("ddve"), DD0 + 64)
                        for tl in range(8):
                            if tl >= 1:
                                sy.wait_ge(SEM("dact"), DA0 + 2 * (tl - 1) + 2)
                            sy.dma_start(out=fin[tl % 2],
                                         in_=d_s4[t0 + tl * 128:t0 + (tl + 1) * 128, :]) \
                                .then_inc(SEM("s4r%d" % (tl % 2)), 16)
                            sy.wait_ge(SEM("dact"), DA0 + 2 * tl + 2)
                            sy.dma_start(out=d_out[t0 + tl * 128:t0 + (tl + 1) * 128, :],
                                         in_=ob[tl % 2]) \
                                .then_inc(SEM("outw%d" % (tl % 2)), 16)
                        if H == 1:
                            sy.wait_ge(SEM("x2w0"), 16 * 8)
                            sy.wait_ge(SEM("x2w1"), 16 * 8)
                            sy.wait_ge(SEM("x2bfw0"), 16 * 8)
                            sy.wait_ge(SEM("x2bfw1"), 16 * 8)
                            sy.wait_ge(SEM("outw0"), 16 * 8)
                            sy.wait_ge(SEM("outw1"), 16 * 8)

                    @blk.tensor
                    def _(te):
                        for hs in range(4):
                            for b in range(8):
                                bidx = hs * 8 + b
                                te.wait_ge(SEM("dwd%d" % (bidx % 2)),
                                           16 * (16 * H + bidx // 2 + 1))
                                for tl in range(8):
                                    if hs >= 1 and b == 0:
                                        te.wait_ge(SEM("ddve"),
                                                   DD0 + 16 * (hs - 1) + tl + 1)
                                    for q in range(8):
                                        gg = b * 8 + q
                                        ins = te.matmul(
                                            pstl(tl),
                                            hhT[:, gg * 1024 + tl * 128:
                                                gg * 1024 + (tl + 1) * 128],
                                            wdr[:, (bidx % 2) * 4096 + q * 512:
                                                (bidx % 2) * 4096 + (q + 1) * 512],
                                            start=(b == 0 and q == 0),
                                            stop=(b == 7 and q == 7))
                                        if tl == 7 and q == 7:
                                            ins.then_inc(SEM("dpe"), 1)

                    @blk.vector
                    def _(v):
                        for hs in range(4):
                            v.wait_ge(SEM("dpe"), PD0X + 8 * (hs + 1))
                            v.wait_ge(SEM("x2r"), 16 * (X2R0 + 8 * (hs + 1)))
                            for tl in range(8):
                                v.scalar_tensor_tensor(
                                    evb(tl), pstl(tl), 1.0, evb(tl),
                                    ALU.mult, ALU.add) \
                                    .then_inc(SEM("ddve"), 1)
                            v.wait_ge(SEM("ddve"), DD0 + 16 * hs + 8)
                            for tl in range(8):
                                v.bn_stats(pst[:, tl * 24 + hs * 6:tl * 24 + hs * 6 + 6],
                                           evb(tl)) \
                                    .then_inc(SEM("ddve"), 1)
                        v.wait_ge(SEM("ddve"), DD0 + 64)
                        for tl in range(8):
                            v.bn_aggr(ag4[:, 2 * tl:2 * tl + 2],
                                      pst[:, tl * 24:(tl + 1) * 24]) \
                                .then_inc(SEM("ddve"), 1)
                            v.wait_ge(SEM("ddve"), DD0 + 64 + 2 * tl + 1)
                            v.wait_ge(SEM("dact"), DA0 + 2 * tl + 1)
                            v.scalar_tensor_tensor(rn4[:, 8 + tl:8 + tl + 1],
                                                   ag4[:, 2 * tl:2 * tl + 1], -1.0,
                                                   rn4[:, tl:tl + 1],
                                                   ALU.mult, ALU.mult) \
                                .then_inc(SEM("ddve"), 1)

                    @blk.scalar
                    def _(sc):
                        if H == 0:
                            for j in range(16):
                                sc.dma_start(out=x2T[:, j * 1024:(j + 1) * 1024],
                                             in_=d_x2bf[1024:2048,
                                                        j * 128:(j + 1) * 128],
                                             transpose=True).then_inc(SEM("ctp"), 16)
                        for tl in range(8):
                            sc.wait_ge(SEM("ddve"), DD0 + 64 + 2 * tl + 1)
                            GRD(sc, "av", sc.activation(acD[:, 0:1],
                                                         ag4[:, 2 * tl + 1:2 * tl + 2],
                                                         AF.Ln, bias=eps_sb[:, 0:1]))
                            sc.activation(rn4[:, tl:tl + 1], acD[:, 0:1], AF.Exp,
                                          scale=-0.5).then_inc(SEM("dact"), 1)
                            sc.wait_ge(SEM("dact"), DA0 + 2 * tl + 1)
                            sc.wait_ge(SEM("ddve"), DD0 + 64 + 2 * tl + 2)
                            sc.wait_ge(SEM("s4r%d" % (tl % 2)),
                                       16 * (4 * H + tl // 2 + 1))
                            if tl >= 1:
                                sc.wait_ge(SEM("outw%d" % ((tl - 1) % 2)),
                                           16 * (4 * H + (tl - 1) // 2 + 1))
                            sc.activation(ob[tl % 2], fin[tl % 2], AF.Identity,
                                          bias=rn4[:, 8 + tl:8 + tl + 1],
                                          scale=rn4[:, tl:tl + 1]) \
                                .then_inc(SEM("dact"), 1)

    return nc


_NC_CACHE = {}


def _get_nc():
    if "nc" not in _NC_CACHE:
        _NC_CACHE["nc"] = build_nc()
    return _NC_CACHE["nc"]


def kernel(L0_attn_x, L0_attn_ln_gamma, L0_attn_ln_beta, L0_attn_W_in,
           L0_attn_b_in, L0_attn_decay_logit, L0_attn_W_out, L0_attn_b_out,
           L0_attn_D_skip, L0_attn_W_down, L0_attn_b_down, L0_attn_W_up,
           L0_attn_b_up, L0_ffn_w_gate_up, L0_ffn_w_down):
    x = np.ascontiguousarray(np.asarray(L0_attn_x, dtype=np.float32))
    gam = np.asarray(L0_attn_ln_gamma, dtype=np.float32)
    bet = np.asarray(L0_attn_ln_beta, dtype=np.float32)
    win = np.asarray(L0_attn_W_in, dtype=np.float32)
    b_in = np.asarray(L0_attn_b_in, dtype=np.float32)
    dlg = np.asarray(L0_attn_decay_logit, dtype=np.float32)
    wout = np.asarray(L0_attn_W_out, dtype=np.float32)
    bout = np.asarray(L0_attn_b_out, dtype=np.float32)
    dsk = np.asarray(L0_attn_D_skip, dtype=np.float32)
    wdn = np.asarray(L0_attn_W_down, dtype=np.float32)
    bdn = np.asarray(L0_attn_b_down, dtype=np.float32)
    wup = np.asarray(L0_attn_W_up, dtype=np.float32)
    bup = np.asarray(L0_attn_b_up, dtype=np.float32)
    wg = np.asarray(L0_ffn_w_gate_up, dtype=np.float32)
    wd = np.asarray(L0_ffn_w_down, dtype=np.float32)

    wg_t = np.ascontiguousarray(
        wg.reshape(128, 128, 16, 128).transpose(0, 3, 2, 1).reshape(128, 128, 2048))
    wd_t = np.ascontiguousarray(
        np.ascontiguousarray(wd.T).reshape(64, 128, 4, 512).transpose(2, 0, 1, 3))
    win_t = np.ascontiguousarray(
        win.reshape(16, 128, 8).transpose(1, 0, 2).reshape(128, 128))
    wdn_t = np.ascontiguousarray(
        wdn.reshape(16, 128, 32).transpose(1, 0, 2).reshape(128, 512))
    wup_aug = np.ascontiguousarray(np.concatenate([wup, bup[None, :]], axis=0))
    gamma_t = np.ascontiguousarray(gam.reshape(16, 128).T)
    beta_t = np.ascontiguousarray(bet.reshape(16, 128).T)
    gamma_rep = np.ascontiguousarray(np.broadcast_to(gam[None, :], (128, DM)))
    dsk_rep = np.ascontiguousarray(np.broadcast_to(dsk[None, :], (128, DM)))
    bout_rep = np.ascontiguousarray(np.broadcast_to(bout[None, :], (128, DM)))
    beta_rep = np.ascontiguousarray(np.broadcast_to(bet[None, :], (128, DM)))
    ident = np.eye(128, dtype=np.float32)

    shared = {
        "wg_t": wg_t, "wd_t": wd_t, "win_t": win_t, "wdn_t": wdn_t,
        "wout": np.ascontiguousarray(wout), "wup_aug": wup_aug,
        "gamma_t": gamma_t, "beta_t": beta_t,
        "gamma_rep": gamma_rep, "dsk_rep": dsk_rep,
        "bout_rep": bout_rep, "beta_rep": beta_rep,
        "b_in": np.ascontiguousarray(b_in.reshape(8, 1)),
        "b_down": np.ascontiguousarray(bdn.reshape(32, 1)),
        "dlogit": np.ascontiguousarray(dlg.reshape(8, 1)),
        "ident": ident,
    }
    in_maps = [dict(shared, x=np.ascontiguousarray(x[b])) for b in range(8)]
    nc = _get_nc()
    res = run_bass_kernel_spmd(nc, in_maps, core_ids=list(range(8)))
    return np.stack([res.results[b]["out"] for b in range(8)], axis=0)


# version 26 (final)
# speedup vs baseline: 1.0166x; 1.0166x over previous
import numpy as np
from contextlib import ExitStack

import concourse.bass as bass
import concourse.mybir as mybir
from concourse.bass_utils import run_bass_kernel_spmd

F32 = mybir.dt.float32
BF16 = mybir.dt.bfloat16
AF = mybir.ActivationFunctionType
ALU = mybir.AluOpType
AX = mybir.AxisListType.X

S = 2048
DM = 2048
T = 16
EPS = 1e-5
INV_N = 1.0 / DM

A_ACT = 6
A_DVE = 2
B_ACT = 9
B_DVE = 14
B_PE = 4
B_POOL = 3


def build_nc():
    nc = bass.Bass("TRN2", target_bir_lowering=False)

    d_x = nc.dram_tensor("x", [S, DM], F32, kind="ExternalInput").ap()
    d_wg = nc.dram_tensor("wg_t", [128, 128, 2048], F32, kind="ExternalInput").ap()
    d_wd = nc.dram_tensor("wd_t", [4, 64, 128, 512], F32, kind="ExternalInput").ap()
    d_win = nc.dram_tensor("win_t", [128, 128], F32, kind="ExternalInput").ap()
    d_wdn = nc.dram_tensor("wdn_t", [128, 512], F32, kind="ExternalInput").ap()
    d_wout = nc.dram_tensor("wout", [8, DM], F32, kind="ExternalInput").ap()
    d_wup = nc.dram_tensor("wup_aug", [33, DM], F32, kind="ExternalInput").ap()
    d_gam = nc.dram_tensor("gamma_t", [128, 16], F32, kind="ExternalInput").ap()
    d_bet = nc.dram_tensor("beta_t", [128, 16], F32, kind="ExternalInput").ap()
    d_gamr = nc.dram_tensor("gamma_rep", [128, DM], F32, kind="ExternalInput").ap()
    d_dskr = nc.dram_tensor("dsk_rep", [128, DM], F32, kind="ExternalInput").ap()
    d_boutr = nc.dram_tensor("bout_rep", [128, DM], F32, kind="ExternalInput").ap()
    d_betr = nc.dram_tensor("beta_rep", [128, DM], F32, kind="ExternalInput").ap()
    d_bin = nc.dram_tensor("b_in", [8, 1], F32, kind="ExternalInput").ap()
    d_bdn = nc.dram_tensor("b_down", [32, 1], F32, kind="ExternalInput").ap()
    d_dlg = nc.dram_tensor("dlogit", [8, 1], F32, kind="ExternalInput").ap()
    d_id = nc.dram_tensor("ident", [128, 128], F32, kind="ExternalInput").ap()

    d_x2 = nc.dram_tensor("x2_scr", [S, DM], F32, kind="Internal").ap()
    d_x2bf = nc.dram_tensor("x2bf_scr", [S, DM], BF16, kind="Internal").ap()
    d_s4 = nc.dram_tensor("s4_scr", [S, DM], F32, kind="Internal").ap()
    d_out = nc.dram_tensor("out", [S, DM], F32, kind="ExternalOutput").ap()

    es = ExitStack()
    with es:
        ec = es.enter_context
        PSA = ec(nc.psum_tensor("PSA", [128, 2048], F32))
        PSB = ec(nc.psum_tensor("PSB", [128, 1024], F32))
        PSC = ec(nc.psum_tensor("PSC", [128, 1024], F32))

        names = (["p0", "p0s", "p0d",
                  "ax0", "ax1", "adve", "aact", "ape",
                  "bx0", "bx1", "bdve", "bact", "bpe", "bpool", "brep",
                  "x2w0", "x2w1", "x2bfw0", "x2bfw1",
                  "ctp", "cpe", "cact", "cdve",
                  "cwg0", "cwg1", "cwg2", "cwg3",
                  "dpe", "ddve", "dact", "x2r",
                  "s4w0", "s4w1", "s4r0", "s4r1", "outw0", "outw1", "vsync",
                  "vv", "pv", "av", "xnw0", "xnw1"]
                 + ["dwd%d" % i for i in range(8)])
        sems = {n: ec(nc.semaphore(n)) for n in names}

        def SEM(n):
            return sems[n]

        _gc = {"vv": [0], "pv": [0], "av": [0]}

        def GRD(eng, name, ins):
            c = _gc[name]
            c[0] += 1
            ins.then_inc(SEM(name), 1)
            eng.wait_ge(SEM(name), c[0])

        eps_sb = ec(nc.sbuf_tensor("eps_sb", [128, 1], F32))
        x2T = ec(nc.sbuf_tensor("x2T", [128, 16 * 1024], BF16))


        ab_es = ExitStack()
        with ab_es:
            ab = ab_es.enter_context
            ident = ab(nc.sbuf_tensor("ident_sb", [128, 128], F32))
            wtil = ab(nc.sbuf_tensor("wtil_bf", [128, 128], BF16))
            wdnt = ab(nc.sbuf_tensor("wdnt_bf", [128, 512], BF16))
            wout_bf = ab(nc.sbuf_tensor("wout_bf", [8, DM], BF16))
            wup_bf = ab(nc.sbuf_tensor("wup_bf", [33, DM], BF16))
            c2 = ab(nc.sbuf_tensor("c2", [8, 1], F32))
            c2d = ab(nc.sbuf_tensor("c2d", [32, 1], F32))
            hbf = ab(nc.sbuf_tensor("hbf", [8, S], BF16))
            hfT = ab(nc.sbuf_tensor("hfT_aug", [33, 128], BF16))
            ag1 = ab(nc.sbuf_tensor("ag1", [128, 2 * T], F32))
            bnst = ab(nc.sbuf_tensor("bnst", [128, 24], F32))
            rsds1 = ab(nc.sbuf_tensor("rsds1", [128, T], F32))
            nmrs1 = ab(nc.sbuf_tensor("nmrs1", [128, T], F32))
            tiny = ab(nc.sbuf_tensor("tiny", [128, 8], F32))
            actA = ab(nc.sbuf_tensor("actA", [128, 2], F32))

            p0_es = ExitStack()
            with p0_es, nc.named_scope("P0"):
                pc = p0_es.enter_context
                win_f = pc(nc.sbuf_tensor("win_f", [128, 128], F32))
                wdn_f = pc(nc.sbuf_tensor("wdn_f", [128, 512], F32))
                win_bf = pc(nc.sbuf_tensor("win_bf", [128, 128], BF16))
                wdn_bf = pc(nc.sbuf_tensor("wdn_bf", [128, 512], BF16))
                gam_t = pc(nc.sbuf_tensor("gam_t", [128, 16], F32))
                bet_t = pc(nc.sbuf_tensor("bet_t", [128, 16], F32))
                bet_bf = pc(nc.sbuf_tensor("bet_bf", [128, 16], BF16))
                wout_f = pc(nc.sbuf_tensor("wout_f", [8, DM], F32))
                wup_f = pc(nc.sbuf_tensor("wup_f", [33, DM], F32))
                sm8 = pc(nc.sbuf_tensor("sm8", [32, 4], F32))
                blk = pc(nc.Block())

                @blk.sync
                def _(sy):
                    for ap_d, ap_s in [(ident[:], d_id), (win_f[:], d_win),
                                       (wdn_f[:], d_wdn), (gam_t[:], d_gam),
                                       (bet_t[:], d_bet), (wout_f[:], d_wout),
                                       (wup_f[:], d_wup), (sm8[0:8, 0:1], d_bin),
                                       (sm8[0:32, 1:2], d_bdn), (sm8[0:8, 2:3], d_dlg)]:
                        sy.dma_start(out=ap_d, in_=ap_s).then_inc(SEM("p0d"), 16)

                @blk.vector
                def _(v):
                    v.wait_ge(SEM("p0d"), 160)
                    v.tensor_copy(bet_bf[:], bet_t[:])
                    v.tensor_copy(win_bf[:], win_f[:])
                    v.tensor_copy(wdn_bf[:], wdn_f[:])
                    v.tensor_copy(wout_bf[:], wout_f[:])
                    v.tensor_copy(wup_bf[:], wup_f[:])
                    w3i = win_f[:].rearrange("p (j c) -> p j c", c=8)
                    w3o = wtil[:].rearrange("p (j c) -> p j c", c=8)
                    for c in range(8):
                        v.tensor_tensor(w3o[:, :, c], w3i[:, :, c], gam_t[:], ALU.mult)
                    d3i = wdn_f[:].rearrange("p (j c) -> p j c", c=32)
                    d3o = wdnt[:].rearrange("p (j c) -> p j c", c=32)
                    for c in range(32):
                        v.tensor_tensor(d3o[:, :, c], d3i[:, :, c], gam_t[:], ALU.mult)
                    v.memset(eps_sb[:], EPS)
                    v.memset(hfT[:], 1.0).then_inc(SEM("p0"), 1)
                    v.wait_ge(SEM("p0s"), 1)
                    v.nop().then_inc(SEM("p0"), 1)
                    v.wait_ge(SEM("p0"), 3)
                    v.tensor_tensor(c2[:], PSB[0:8, 0:1], sm8[0:8, 0:1], ALU.add) \
                        .then_inc(SEM("p0"), 1)
                    v.wait_ge(SEM("p0"), 5)
                    v.tensor_tensor(c2d[:], PSB[0:32, 0:1], sm8[0:32, 1:2], ALU.add) \
                        .then_inc(SEM("p0"), 1)

                @blk.scalar
                def _(sc):
                    sc.wait_ge(SEM("p0d"), 160)
                    sc.nop().then_inc(SEM("p0s"), 1)

                @blk.tensor
                def _(te):
                    te.wait_ge(SEM("p0"), 1)
                    for j in range(16):
                        ins = te.matmul(PSB[0:8, 0:1], win_bf[:, j * 8:(j + 1) * 8],
                                        bet_bf[:, j:j + 1],
                                        start=(j == 0), stop=(j == 15))
                        if j == 15:
                            ins.then_inc(SEM("p0"), 1)
                    te.wait_ge(SEM("p0"), 4)
                    for j in range(16):
                        ins = te.matmul(PSB[0:32, 0:1], wdn_bf[:, j * 32:(j + 1) * 32],
                                        bet_bf[:, j:j + 1],
                                        start=(j == 0), stop=(j == 15))
                        if j == 15:
                            ins.then_inc(SEM("p0"), 1)

            a_es = ExitStack()
            with a_es, nc.named_scope("PA"):
                ac_ = a_es.enter_context
                xa0 = ac_(nc.sbuf_tensor("xbA0", [128, DM], F32))
                xa1 = ac_(nc.sbuf_tensor("xbA1", [128, DM], F32))
                xna0 = ac_(nc.sbuf_tensor("xnpA0", [128, DM], BF16))
                xna1 = ac_(nc.sbuf_tensor("xnpA1", [128, DM], BF16))
                xnTF = ac_(nc.sbuf_tensor("xnT_full", [128, 16 * DM], BF16))
                u_sb = ac_(nc.sbuf_tensor("u_sb", [8, S], F32))
                h_sb = ac_(nc.sbuf_tensor("h_sb", [8, S], F32))
                afull = ac_(nc.sbuf_tensor("afull", [8, S], F32))
                blk = ac_(nc.Block())
                xb = [xa0, xa1]
                xnp = [xna0, xna1]

                @blk.sync
                def _(sy):
                    for i in range(T + 1):
                        if i < T:
                            if i >= 2:
                                sy.wait_ge(SEM("aact"), 4 * (i - 2) + 4)
                            sy.dma_start(out=xb[i % 2][:],
                                         in_=d_x[i * 128:(i + 1) * 128, :]) \
                                .then_inc(SEM("ax%d" % (i % 2)), 16)
                        if i >= 1:
                            j = i - 1
                            sy.wait_ge(SEM("aact"), 4 * j + 4)
                            sy.dma_start(out=d_x2bf[j * 128:(j + 1) * 128, :],
                                         in_=xnp[j % 2][:]) \
                                .then_inc(SEM("xnw%d" % (j % 2)), 16)

                @blk.scalar
                def _(sc):
                    sc.dma_start(out=actA[0:8, 0:1], in_=d_dlg).then_inc(SEM("p0d"), 16)
                    sc.wait_ge(SEM("p0d"), 176)
                    sc.activation(actA[0:8, 1:2], actA[0:8, 0:1], AF.Sigmoid) \
                        .then_inc(SEM("p0s"), 1)
                    for i in range(T):
                        par = i % 2
                        sc.wait_ge(SEM("adve"), A_DVE * i + 1)
                        sc.activation(actA[:, 0:1], ag1[:, 2 * i + 1:2 * i + 2],
                                      AF.Ln, bias=eps_sb[:, 0:1]) \
                            .then_inc(SEM("aact"), 1)
                        sc.wait_ge(SEM("aact"), 4 * i + 1)
                        sc.activation(rsds1[:, i:i + 1], actA[:, 0:1], AF.Exp,
                                      scale=-0.5).then_inc(SEM("aact"), 2)
                        sc.wait_ge(SEM("adve"), A_DVE * i + 2)
                        if i >= 2:
                            sc.wait_ge(SEM("xnw%d" % (i % 2)), 16 * (i // 2))
                        sc.activation(xnp[par][:], xb[par][:], AF.Identity,
                                      bias=nmrs1[:, i:i + 1],
                                      scale=rsds1[:, i:i + 1]).then_inc(SEM("aact"), 1)
                    sc.wait_ge(SEM("xnw0"), 16 * 8)
                    sc.wait_ge(SEM("xnw1"), 16 * 8)
                    for j in range(16):
                        sc.dma_start(out=xnTF[:, j * DM:(j + 1) * DM],
                                     in_=d_x2bf[0:S, j * 128:(j + 1) * 128],
                                     transpose=True).then_inc(SEM("ctp"), 16)
                    for tc in range(4):
                        sc.wait_ge(SEM("ape"), tc + 1)
                        sc.activation(u_sb[0:8, tc * 512:(tc + 1) * 512],
                                      PSB[0:8, (tc % 2) * 512:(tc % 2) * 512 + 512],
                                      AF.Identity, bias=c2[:, 0:1]) \
                            .then_inc(SEM("aact"), 1)

                @blk.vector
                def _(v):
                    v.wait_ge(SEM("p0s"), 2)
                    GRD(v, "vv", v.memset(afull[:], 1.0))
                    v.tensor_scalar_mul(afull[:], afull[:], actA[0:8, 1:2])
                    for i in range(T):
                        par = i % 2
                        v.wait_ge(SEM("ax%d" % par), 16 * (i // 2 + 1))
                        X = xb[par][:]
                        for k in range(4):
                            ins = v.bn_stats(bnst[:, k * 6:(k + 1) * 6],
                                             X[:, k * 512:(k + 1) * 512])
                        GRD(v, "vv", ins)
                        v.bn_aggr(ag1[:, 2 * i:2 * i + 2], bnst[:]) \
                            .then_inc(SEM("adve"), 1)
                        v.wait_ge(SEM("adve"), A_DVE * i + 1)
                        v.wait_ge(SEM("aact"), 4 * i + 3)
                        v.scalar_tensor_tensor(nmrs1[:, i:i + 1],
                                               ag1[:, 2 * i:2 * i + 1],
                                               -1.0, rsds1[:, i:i + 1],
                                               ALU.mult, ALU.mult) \
                            .then_inc(SEM("adve"), 1)
                    v.wait_ge(SEM("aact"), 4 * T + 4)
                    GRD(v, "vv", v.tensor_tensor_scan(h_sb[:], afull[:], u_sb[:], 0.0,
                                                      ALU.mult, ALU.add))
                    v.tensor_copy(hbf[:], h_sb[:]).then_inc(SEM("adve"), 1)

                @blk.tensor
                def _(te):
                    te.wait_ge(SEM("ctp"), 256)
                    for tc in range(4):
                        if tc >= 2:
                            te.wait_ge(SEM("aact"), 4 * T + (tc - 2) + 1)
                        for j in range(16):
                            ins = te.matmul(
                                PSB[0:8, (tc % 2) * 512:(tc % 2) * 512 + 512],
                                wtil[:, j * 8:(j + 1) * 8],
                                xnTF[:, j * DM + tc * 512:j * DM + (tc + 1) * 512],
                                start=(j == 0), stop=(j == 15))
                            if j == 15:
                                ins.then_inc(SEM("ape"), 1)

            b_es = ExitStack()
            with b_es, nc.named_scope("PB"):
                bc = b_es.enter_context
                xbb0 = bc(nc.sbuf_tensor("xbB0", [128, DM], F32))
                xbb1 = bc(nc.sbuf_tensor("xbB1", [128, DM], F32))
                xnb0 = bc(nc.sbuf_tensor("xnpB0", [128, DM], F32))
                xnb1 = bc(nc.sbuf_tensor("xnpB1", [128, DM], F32))
                sp0 = bc(nc.sbuf_tensor("scrp0", [128, DM], F32))
                sp1 = bc(nc.sbuf_tensor("scrp1", [128, DM], F32))
                x10 = bc(nc.sbuf_tensor("x1B0", [128, DM], F32))
                x11 = bc(nc.sbuf_tensor("x1B1", [128, DM], F32))
                q20 = bc(nc.sbuf_tensor("xn2p0", [128, DM], F32))
                q21 = bc(nc.sbuf_tensor("xn2p1", [128, DM], F32))
                qt0 = bc(nc.sbuf_tensor("xn2T0", [128, DM], BF16))
                qt1 = bc(nc.sbuf_tensor("xn2T1", [128, DM], BF16))
                sf0 = bc(nc.sbuf_tensor("scrf0", [128, DM], F32))
                sf1 = bc(nc.sbuf_tensor("scrf1", [128, DM], F32))
                ss0 = bc(nc.sbuf_tensor("sB0", [128, DM], F32))
                ss1 = bc(nc.sbuf_tensor("sB1", [128, DM], F32))
                x20 = bc(nc.sbuf_tensor("x2B0", [128, DM], F32))
                x21 = bc(nc.sbuf_tensor("x2B1", [128, DM], F32))
                xf0 = bc(nc.sbuf_tensor("x2bf0", [128, DM], BF16))
                xf1 = bc(nc.sbuf_tensor("x2bf1", [128, DM], BF16))
                gd_rep = bc(nc.sbuf_tensor("gd_rep", [128, DM], F32))
                ball_rep = bc(nc.sbuf_tensor("ball_rep", [128, DM], F32))
                ag2 = bc(nc.sbuf_tensor("ag2", [128, 2 * T], F32))
                ag3 = bc(nc.sbuf_tensor("ag3", [128, 2 * T], F32))
                rn2 = bc(nc.sbuf_tensor("rn2", [128, 2 * T], F32))
                rn3 = bc(nc.sbuf_tensor("rn3", [128, 2 * T], F32))
                actB = bc(nc.sbuf_tensor("actB", [128, 2], F32))
                blk = bc(nc.Block())
                bxb = [xbb0, xbb1]
                bxnp = [xnb0, xnb1]
                scrp = [sp0, sp1]
                x1b = [x10, x11]
                xn2p = [q20, q21]
                xn2T = [qt0, qt1]
                scrf = [sf0, sf1]
                s_b = [ss0, ss1]
                x2b = [x20, x21]
                x2bf = [xf0, xf1]

                @blk.sync
                def _(sy):
                    sy.dma_start(out=gd_rep[:], in_=d_gamr).then_inc(SEM("brep"), 16)
                    sy.dma_start(out=sp0[:], in_=d_dskr).then_inc(SEM("brep"), 16)
                    sy.dma_start(out=sp1[:], in_=d_boutr).then_inc(SEM("brep"), 16)
                    sy.dma_start(out=ball_rep[:], in_=d_betr).then_inc(SEM("brep"), 16)
                    for i in range(T + 2):
                        if i < T:
                            if i >= 2:
                                sy.wait_ge(SEM("bpool"), B_POOL * (i - 2) + 3)
                            sy.dma_start(out=bxb[i % 2][:],
                                         in_=d_x[i * 128:(i + 1) * 128, :]) \
                                .then_inc(SEM("bx%d" % (i % 2)), 16)
                        if i >= 2:
                            j = i - 2
                            sy.wait_ge(SEM("bact"), 9 * j + 10)
                            sy.dma_start(out=d_x2[j * 128:(j + 1) * 128, :],
                                         in_=x2b[j % 2][:]) \
                                .then_inc(SEM("x2w%d" % (j % 2)), 16)

                @blk.gpsimd
                def _(g):
                    g.wait_ge(SEM("brep"), 64)
                    GRD(g, "pv", g.tensor_tensor(gd_rep[:], gd_rep[:],
                                                 sp0[:], ALU.mult))
                    GRD(g, "pv", g.tensor_tensor(ball_rep[:], ball_rep[:],
                                                 sp0[:], ALU.mult))
                    g.tensor_tensor(ball_rep[:], ball_rep[:], sp1[:], ALU.add) \
                        .then_inc(SEM("brep"), 1)
                    for i in range(T):
                        par = i % 2
                        g.wait_ge(SEM("bact"), 1 if i == 0 else 9 * i - 2)
                        if i >= 2:
                            g.wait_ge(SEM("bdve"), B_DVE * (i - 2) + 2)
                        GRD(g, "pv", g.tensor_tensor(scrp[par][:], bxnp[par][:],
                                                      gd_rep[:], ALU.mult))
                        g.tensor_tensor(scrp[par][:], scrp[par][:], ball_rep[:],
                                        ALU.add).then_inc(SEM("bpool"), 2)
                        g.wait_ge(SEM("bdve"), B_DVE * i + 2)
                        if i >= 2:
                            g.wait_ge(SEM("bdve"), B_DVE * (i - 2) + 8)
                        g.tensor_tensor(scrf[par][:], bxb[par][:], x1b[par][:],
                                        ALU.add).then_inc(SEM("bpool"), 1)

                @blk.vector
                def _(v):
                    for i in range(T):
                        par = i % 2
                        X = bxb[par][:]
                        v.wait_ge(SEM("bpe"), B_PE * i + 1)
                        v.wait_ge(SEM("bx%d" % par), 16 * (i // 2 + 1))
                        if i >= 2:
                            v.wait_ge(SEM("bact"), 9 * (i - 2) + 4)
                            v.wait_ge(SEM("bpool"), B_POOL * (i - 2) + 3)
                        v.scalar_tensor_tensor(x1b[par][:], PSA[:], 1.0, X,
                                               ALU.mult, ALU.add) \
                            .then_inc(SEM("bdve"), 1)
                        v.wait_ge(SEM("bdve"), B_DVE * i + 1)
                        v.wait_ge(SEM("bpool"), B_POOL * i + 2)
                        v.tensor_tensor(x1b[par][:], x1b[par][:],
                                        scrp[par][:], ALU.add) \
                            .then_inc(SEM("bdve"), 1)
                        v.wait_ge(SEM("bdve"), B_DVE * i + 2)
                        v.wait_ge(SEM("bdve"), B_DVE * i + 2)
                        for k in range(4):
                            ins = v.bn_stats(bnst[:, k * 6:(k + 1) * 6],
                                             x1b[par][:, k * 512:(k + 1) * 512])
                        GRD(v, "vv", ins)
                        v.bn_aggr(ag2[:, 2 * i:2 * i + 2], bnst[:]) \
                            .then_inc(SEM("bdve"), 4)
                        v.wait_ge(SEM("bact"), 9 * i + 3)
                        v.scalar_tensor_tensor(rn2[:, T + i:T + i + 1],
                                               ag2[:, 2 * i:2 * i + 1], -1.0,
                                               rn2[:, i:i + 1], ALU.mult, ALU.mult) \
                            .then_inc(SEM("bdve"), 1)
                        v.wait_ge(SEM("bpe"), B_PE * i + 4)
                        v.wait_ge(SEM("bpool"), B_POOL * i + 3)
                        if i >= 2:
                            v.wait_ge(SEM("bact"), 9 * (i - 2) + 10)
                        v.scalar_tensor_tensor(s_b[par][:, 0:1024], PSB[:], 1.0,
                                               scrf[par][:, 0:1024],
                                               ALU.mult, ALU.add)
                        v.scalar_tensor_tensor(s_b[par][:, 1024:2048], PSC[:], 1.0,
                                               scrf[par][:, 1024:2048],
                                               ALU.mult, ALU.add) \
                            .then_inc(SEM("bdve"), 1)
                        v.wait_ge(SEM("bdve"), B_DVE * i + 8)
                        for k in range(4):
                            ins = v.bn_stats(bnst[:, k * 6:(k + 1) * 6],
                                             s_b[par][:, k * 512:(k + 1) * 512])
                        GRD(v, "vv", ins)
                        v.bn_aggr(ag3[:, 2 * i:2 * i + 2], bnst[:]) \
                            .then_inc(SEM("bdve"), 4)
                        v.wait_ge(SEM("bact"), 9 * i + 9)
                        v.scalar_tensor_tensor(rn3[:, T + i:T + i + 1],
                                               ag3[:, 2 * i:2 * i + 1], -1.0,
                                               rn3[:, i:i + 1], ALU.mult, ALU.mult) \
                            .then_inc(SEM("bdve"), 1)
                        v.wait_ge(SEM("bact"), 9 * i + 10)
                        if i >= 2:
                            v.wait_ge(SEM("x2bfw%d" % par), 16 * (i // 2))
                        v.tensor_copy(x2bf[par][:], x2b[par][:]) \
                            .then_inc(SEM("bdve"), 1)

                @blk.scalar
                def _(sc):
                    sc.wait_ge(SEM("bx0"), 16)
                    sc.wait_ge(SEM("brep"), 65)
                    sc.activation(bxnp[0][:], bxb[0][:], AF.Identity,
                                  bias=nmrs1[:, 0:1],
                                  scale=rsds1[:, 0:1]).then_inc(SEM("bact"), 1)
                    for i in range(T):
                        par = i % 2
                        sc.wait_ge(SEM("bdve"), B_DVE * i + 6)
                        sc.activation(actB[:, 0:1], ag2[:, 2 * i + 1:2 * i + 2],
                                      AF.Ln, bias=eps_sb[:, 0:1]) \
                            .then_inc(SEM("bact"), 1)
                        sc.wait_ge(SEM("bact"), 9 * i + 2)
                        sc.activation(rn2[:, i:i + 1], actB[:, 0:1], AF.Exp,
                                      scale=-0.5).then_inc(SEM("bact"), 1)
                        sc.wait_ge(SEM("bdve"), B_DVE * i + 7)
                        if i >= 2:
                            sc.wait_ge(SEM("bpe"), B_PE * (i - 2) + 2)
                        sc.activation(xn2p[par][:], x1b[par][:], AF.Identity,
                                      bias=rn2[:, T + i:T + i + 1],
                                      scale=rn2[:, i:i + 1]).then_inc(SEM("bact"), 1)
                        sc.wait_ge(SEM("bpe"), B_PE * i + 2)
                        sc.activation(xn2T[par][:], PSA[:], AF.Copy) \
                            .then_inc(SEM("bact"), 1)
                        sc.wait_ge(SEM("bpe"), B_PE * i + 3)
                        sc.activation(hfT[0:32, :], PSA[0:32, 0:128], AF.Relu,
                                      bias=c2d[:, 0:1])
                        sc.drain().then_inc(SEM("bact"), 1)
                        if i < T - 1:
                            np_ = (i + 1) % 2
                            sc.wait_ge(SEM("bx%d" % np_), 16 * ((i + 1) // 2 + 1))
                            if i >= 1:
                                sc.wait_ge(SEM("bpool"), B_POOL * (i - 1) + 1)
                            sc.activation(bxnp[np_][:], bxb[np_][:], AF.Identity,
                                          bias=nmrs1[:, i + 1:i + 2],
                                          scale=rsds1[:, i + 1:i + 2]) \
                                .then_inc(SEM("bact"), 1)
                        else:
                            sc.nop().then_inc(SEM("bact"), 1)
                        sc.wait_ge(SEM("bdve"), B_DVE * i + 12)
                        sc.activation(actB[:, 1:2], ag3[:, 2 * i + 1:2 * i + 2],
                                      AF.Ln, bias=eps_sb[:, 0:1]) \
                            .then_inc(SEM("bact"), 1)
                        sc.wait_ge(SEM("bact"), 9 * i + 8)
                        sc.activation(rn3[:, i:i + 1], actB[:, 1:2], AF.Exp,
                                      scale=-0.5).then_inc(SEM("bact"), 1)
                        sc.wait_ge(SEM("bdve"), B_DVE * i + 13)
                        if i >= 2:
                            sc.wait_ge(SEM("x2w%d" % par), 16 * (i // 2))
                            sc.wait_ge(SEM("bdve"), B_DVE * (i - 2) + 14)
                        sc.activation(x2b[par][:], s_b[par][:], AF.Identity,
                                      bias=rn3[:, T + i:T + i + 1],
                                      scale=rn3[:, i:i + 1]).then_inc(SEM("bact"), 1)
                        sc.wait_ge(SEM("bdve"), B_DVE * i + 14)
                        sc.dma_start(out=d_x2bf[i * 128:(i + 1) * 128, :],
                                     in_=x2bf[par][:]) \
                            .then_inc(SEM("x2bfw%d" % par), 16)
                        if i == 7:
                            sc.wait_ge(SEM("x2bfw0"), 16 * 4)
                            sc.wait_ge(SEM("x2bfw1"), 16 * 4)
                            for j in range(16):
                                sc.dma_start(out=x2T[:, j * 1024:(j + 1) * 1024],
                                             in_=d_x2bf[0:1024,
                                                        j * 128:(j + 1) * 128],
                                             transpose=True).then_inc(SEM("ctp"), 16)

                @blk.tensor
                def _(te):
                    for i in range(T):
                        par = i % 2
                        if i >= 1:
                            te.wait_ge(SEM("bact"), 9 * (i - 1) + 6)
                            te.wait_ge(SEM("bdve"), B_DVE * (i - 1) + 1)
                        for cc in range(4):
                            ins = te.matmul(PSA[:, cc * 512:(cc + 1) * 512],
                                            hbf[0:8, i * 128:(i + 1) * 128],
                                            wout_bf[0:8, cc * 512:(cc + 1) * 512],
                                            start=True, stop=True)
                            if cc == 3:
                                ins.then_inc(SEM("bpe"), 1)
                        te.wait_ge(SEM("bact"), 9 * i + 4)
                        te.wait_ge(SEM("bdve"), B_DVE * i + 1)
                        for k in range(16):
                            ins = te.transpose(PSA[:, k * 128:(k + 1) * 128],
                                               xn2p[par][:, k * 128:(k + 1) * 128],
                                               ident[:])
                            if k == 15:
                                ins.then_inc(SEM("bpe"), 1)
                        te.wait_ge(SEM("bact"), 9 * i + 5)
                        for j in range(16):
                            ins = te.matmul(PSA[0:32, 0:128],
                                            wdnt[:, j * 32:(j + 1) * 32],
                                            xn2T[par][:, j * 128:(j + 1) * 128],
                                            start=(j == 0), stop=(j == 15))
                            if j == 15:
                                ins.then_inc(SEM("bpe"), 1)
                        te.wait_ge(SEM("bact"), 9 * i + 6)
                        if i >= 1:
                            te.wait_ge(SEM("bdve"), B_DVE * (i - 1) + 8)
                        for cc in range(4):
                            dst = PSB[:, (cc % 2) * 512:(cc % 2 + 1) * 512] if cc < 2 \
                                else PSC[:, (cc % 2) * 512:(cc % 2 + 1) * 512]
                            ins = te.matmul(dst,
                                            hfT[0:33, 0:128],
                                            wup_bf[0:33, cc * 512:(cc + 1) * 512],
                                            start=True, stop=True)
                            if cc == 3:
                                ins.then_inc(SEM("bpe"), 1)

        with nc.sbuf_tensor("hhT", [128, 64 * 1024], BF16) as hhT, \
             nc.sbuf_tensor("wdr", [128, 2 * 8 * 512], BF16) as wdr, \
             nc.sbuf_tensor("wgb0", [128, 2048], BF16) as wgb0h, \
             nc.sbuf_tensor("wgb1", [128, 2048], BF16) as wgb1h:
            wgb01 = [wgb0h, wgb1h]
            for H in range(2):
                t0 = H * 1024
                CPE0 = 64 * H
                CA0 = 128 * H
                CD0 = 128 * H

                c_es = ExitStack()
                with c_es, nc.named_scope("PC%d" % H):
                    cc_ = c_es.enter_context
                    wgb2 = cc_(nc.sbuf_tensor("wgb2_%d" % H, [128, 2048], BF16))
                    wgb3 = cc_(nc.sbuf_tensor("wgb3_%d" % H, [128, 2048], BF16))
                    sil0 = cc_(nc.sbuf_tensor("sil0_%d" % H, [128, 1024], F32))
                    sil1 = cc_(nc.sbuf_tensor("sil1_%d" % H, [128, 1024], F32))
                    blk = cc_(nc.Block())
                    wgb = [wgb01[0], wgb01[1], wgb2, wgb3]
                    sil = [sil0, sil1]

                    @blk.scalar
                    def _(sc):
                        for gp in range(64):
                            sc.wait_ge(SEM("cpe"), CPE0 + gp + 1)
                            if gp >= 2:
                                sc.wait_ge(SEM("cdve"), CD0 + 2 * (gp - 2) + 2)
                            pi = gp % 2
                            for c in range(2):
                                sc.activation(sil[pi][:, c * 512:(c + 1) * 512],
                                              PSA[:, pi * 1024 + c * 512:
                                                  pi * 1024 + (c + 1) * 512],
                                              AF.Silu).then_inc(SEM("cact"), 1)

                    @blk.gpsimd
                    def _(g):
                        for idx in range(128):
                            gp, role = idx // 2, idx % 2
                            ot = gp + 64 * role
                            if H == 1 and idx < 2:
                                continue
                            if idx >= 4:
                                g.wait_ge(SEM("cpe"), CPE0 + (idx - 4) // 2 + 1)
                            g.dma_start(out=wgb[idx % 4][:], in_=d_wg[ot]) \
                                .then_inc(SEM("cwg%d" % (idx % 4)), 16)
                        if H == 1:
                            g.wait_ge(SEM("dpe"), 32)
                        for pidx in range(2):
                            g.dma_start(
                                out=wdr[:, pidx * 4096:(pidx + 1) * 4096],
                                in_=d_wd[0, pidx * 8:(pidx + 1) * 8]
                                .rearrange("g p c -> p g c")) \
                                .then_inc(SEM("dwd%d" % pidx, ), 16)

                    @blk.tensor
                    def _(te):
                        te.wait_ge(SEM("ctp"), 256 * (H + 2))
                        for gp in range(64):
                            pi = gp % 2
                            if gp >= 2:
                                te.wait_ge(SEM("cact"), CA0 + 2 * (gp - 2) + 2)
                                te.wait_ge(SEM("cdve"), CD0 + 2 * (gp - 2) + 2)
                            for role in range(2):
                                idx = 2 * gp + role
                                te.wait_ge(SEM("cwg%d" % (idx % 4)),
                                           16 * (32 * H + idx // 4 + 1))
                                dst = PSA[:, pi * 1024:(pi + 1) * 1024] if role == 0 \
                                    else (PSB[:] if pi == 0 else PSC[:])
                                for c in range(2):
                                    for k in range(16):
                                        ins = te.matmul(
                                            dst[:, c * 512:(c + 1) * 512],
                                            wgb[idx % 4][:, k * 128:(k + 1) * 128],
                                            x2T[:, k * 1024 + c * 512:
                                                k * 1024 + (c + 1) * 512],
                                            start=(k == 0), stop=(k == 15))
                                        if role == 1 and c == 1 and k == 15:
                                            ins.then_inc(SEM("cpe"), 1)

                    @blk.vector
                    def _(v):
                        for gp in range(64):
                            pi = gp % 2
                            ups = PSB[:] if pi == 0 else PSC[:]
                            v.wait_ge(SEM("cpe"), CPE0 + gp + 1)
                            for c in range(2):
                                v.wait_ge(SEM("cact"), CA0 + 2 * gp + c + 1)
                                v.tensor_tensor(
                                    hhT[:, gp * 1024 + c * 512:gp * 1024 + (c + 1) * 512],
                                    sil[pi][:, c * 512:(c + 1) * 512],
                                    ups[:, c * 512:(c + 1) * 512], ALU.mult) \
                                    .then_inc(SEM("cdve"), 1)

                WD0 = 256 * H
                PD0 = 256 * H
                PD0X = 32 * H
                DD0 = 80 * H
                X2R0 = 32 * H
                DA0 = 16 * H

                d_es = ExitStack()
                with d_es, nc.named_scope("PD%d" % H):
                    dc = d_es.enter_context
                    evb8 = dc(nc.sbuf_tensor("evb8_%d" % H, [128, 8 * 512], F32))

                    pst = dc(nc.sbuf_tensor("pst_%d" % H, [128, 192], F32))
                    ag4 = dc(nc.sbuf_tensor("ag4_%d" % H, [128, 16], F32))
                    acD = dc(nc.sbuf_tensor("acD_%d" % H, [128, 2], F32))
                    rn4 = dc(nc.sbuf_tensor("rn4_%d" % H, [128, 16], F32))
                    blk = dc(nc.Block())
                    def evb(tl):
                        return evb8[:, tl * 512:(tl + 1) * 512]
                    finv = evb8[:, 0:2048]
                    obv = evb8[:, 2048:4096]
                    fin = [finv, finv]
                    ob = [obv, obv]

                    def pstl(tl):
                        if tl < 4:
                            return PSA[:, tl * 512:(tl + 1) * 512]
                        if tl < 6:
                            return PSB[:, (tl - 4) * 512:(tl - 3) * 512]
                        return PSC[:, (tl - 6) * 512:(tl - 5) * 512]

                    @blk.gpsimd
                    def _(g):
                        for hs in range(4):
                            for b in range(8):
                                idx = hs * 8 + b
                                if idx < 2:
                                    continue
                                g.wait_ge(SEM("dpe"), PD0X + idx - 1)
                                g.dma_start(
                                    out=wdr[:, (idx % 2) * 4096:(idx % 2 + 1) * 4096],
                                    in_=d_wd[hs, b * 8:(b + 1) * 8]
                                    .rearrange("g p c -> p g c")) \
                                    .then_inc(SEM("dwd%d" % (idx % 2)), 16)
                        if H == 0:
                            g.wait_ge(SEM("cpe"), 64)
                            for idx in range(2):
                                g.dma_start(out=wgb[idx][:],
                                            in_=d_wg[64 * idx]) \
                                    .then_inc(SEM("cwg%d" % idx), 16)

                    @blk.sync
                    def _(sy):
                        if H == 0:
                            sy.wait_ge(SEM("x2w0"), 16 * (T // 2))
                            sy.wait_ge(SEM("x2w1"), 16 * (T // 2))
                        for hs in range(4):
                            if hs >= 1:
                                sy.wait_ge(SEM("ddve"), DD0 + 16 * hs)
                            if hs >= 1:
                                sy.wait_ge(SEM("s4w0"), 16 * (32 * H + 8 * hs))
                            for tl in range(8):
                                sy.dma_start(
                                    out=evb(tl),
                                    in_=d_x2[t0 + tl * 128:t0 + (tl + 1) * 128,
                                             hs * 512:(hs + 1) * 512]) \
                                    .then_inc(SEM("x2r"), 16)
                            for tl in range(8):
                                sy.wait_ge(SEM("ddve"), DD0 + 16 * hs + tl + 1)
                                sy.dma_start(
                                    out=d_s4[t0 + tl * 128:t0 + (tl + 1) * 128,
                                             hs * 512:(hs + 1) * 512],
                                    in_=evb(tl)) \
                                    .then_inc(SEM("s4w0"), 16)
                        sy.wait_ge(SEM("s4w0"), 16 * 32 * (H + 1))
                        sy.wait_ge(SEM("ddve"), DD0 + 64)
                        for tl in range(8):
                            if tl >= 1:
                                sy.wait_ge(SEM("dact"), DA0 + 2 * (tl - 1) + 2)
                            sy.dma_start(out=fin[tl % 2],
                                         in_=d_s4[t0 + tl * 128:t0 + (tl + 1) * 128, :]) \
                                .then_inc(SEM("s4r%d" % (tl % 2)), 16)
                            sy.wait_ge(SEM("dact"), DA0 + 2 * tl + 2)
                            sy.dma_start(out=d_out[t0 + tl * 128:t0 + (tl + 1) * 128, :],
                                         in_=ob[tl % 2]) \
                                .then_inc(SEM("outw%d" % (tl % 2)), 16)
                        if H == 1:
                            sy.wait_ge(SEM("x2w0"), 16 * 8)
                            sy.wait_ge(SEM("x2w1"), 16 * 8)
                            sy.wait_ge(SEM("x2bfw0"), 16 * 8)
                            sy.wait_ge(SEM("x2bfw1"), 16 * 8)
                            sy.wait_ge(SEM("outw0"), 16 * 8)
                            sy.wait_ge(SEM("outw1"), 16 * 8)

                    @blk.tensor
                    def _(te):
                        for hs in range(4):
                            for b in range(8):
                                bidx = hs * 8 + b
                                te.wait_ge(SEM("dwd%d" % (bidx % 2)),
                                           16 * (16 * H + bidx // 2 + 1))
                                for tl in range(8):
                                    if hs >= 1 and b == 0:
                                        te.wait_ge(SEM("ddve"),
                                                   DD0 + 16 * (hs - 1) + tl + 1)
                                    for q in range(8):
                                        gg = b * 8 + q
                                        ins = te.matmul(
                                            pstl(tl),
                                            hhT[:, gg * 1024 + tl * 128:
                                                gg * 1024 + (tl + 1) * 128],
                                            wdr[:, (bidx % 2) * 4096 + q * 512:
                                                (bidx % 2) * 4096 + (q + 1) * 512],
                                            start=(b == 0 and q == 0),
                                            stop=(b == 7 and q == 7))
                                        if tl == 7 and q == 7:
                                            ins.then_inc(SEM("dpe"), 1)

                    @blk.vector
                    def _(v):
                        for hs in range(4):
                            v.wait_ge(SEM("dpe"), PD0X + 8 * (hs + 1))
                            v.wait_ge(SEM("x2r"), 16 * (X2R0 + 8 * (hs + 1)))
                            for tl in range(8):
                                v.scalar_tensor_tensor(
                                    evb(tl), pstl(tl), 1.0, evb(tl),
                                    ALU.mult, ALU.add) \
                                    .then_inc(SEM("ddve"), 1)
                            v.wait_ge(SEM("ddve"), DD0 + 16 * hs + 8)
                            for tl in range(8):
                                v.bn_stats(pst[:, tl * 24 + hs * 6:tl * 24 + hs * 6 + 6],
                                           evb(tl)) \
                                    .then_inc(SEM("ddve"), 1)
                        v.wait_ge(SEM("ddve"), DD0 + 64)
                        for tl in range(8):
                            v.bn_aggr(ag4[:, 2 * tl:2 * tl + 2],
                                      pst[:, tl * 24:(tl + 1) * 24]) \
                                .then_inc(SEM("ddve"), 1)
                            v.wait_ge(SEM("ddve"), DD0 + 64 + 2 * tl + 1)
                            v.wait_ge(SEM("dact"), DA0 + 2 * tl + 1)
                            v.scalar_tensor_tensor(rn4[:, 8 + tl:8 + tl + 1],
                                                   ag4[:, 2 * tl:2 * tl + 1], -1.0,
                                                   rn4[:, tl:tl + 1],
                                                   ALU.mult, ALU.mult) \
                                .then_inc(SEM("ddve"), 1)

                    @blk.scalar
                    def _(sc):
                        if H == 0:
                            for j in range(16):
                                sc.dma_start(out=x2T[:, j * 1024:(j + 1) * 1024],
                                             in_=d_x2bf[1024:2048,
                                                        j * 128:(j + 1) * 128],
                                             transpose=True).then_inc(SEM("ctp"), 16)
                        for tl in range(8):
                            sc.wait_ge(SEM("ddve"), DD0 + 64 + 2 * tl + 1)
                            GRD(sc, "av", sc.activation(acD[:, 0:1],
                                                         ag4[:, 2 * tl + 1:2 * tl + 2],
                                                         AF.Ln, bias=eps_sb[:, 0:1]))
                            sc.activation(rn4[:, tl:tl + 1], acD[:, 0:1], AF.Exp,
                                          scale=-0.5).then_inc(SEM("dact"), 1)
                            sc.wait_ge(SEM("dact"), DA0 + 2 * tl + 1)
                            sc.wait_ge(SEM("ddve"), DD0 + 64 + 2 * tl + 2)
                            sc.wait_ge(SEM("s4r%d" % (tl % 2)),
                                       16 * (4 * H + tl // 2 + 1))
                            if tl >= 1:
                                sc.wait_ge(SEM("outw%d" % ((tl - 1) % 2)),
                                           16 * (4 * H + (tl - 1) // 2 + 1))
                            sc.activation(ob[tl % 2], fin[tl % 2], AF.Identity,
                                          bias=rn4[:, 8 + tl:8 + tl + 1],
                                          scale=rn4[:, tl:tl + 1]) \
                                .then_inc(SEM("dact"), 1)

    return nc


_NC_CACHE = {}


def _get_nc():
    if "nc" not in _NC_CACHE:
        _NC_CACHE["nc"] = build_nc()
    return _NC_CACHE["nc"]


def kernel(L0_attn_x, L0_attn_ln_gamma, L0_attn_ln_beta, L0_attn_W_in,
           L0_attn_b_in, L0_attn_decay_logit, L0_attn_W_out, L0_attn_b_out,
           L0_attn_D_skip, L0_attn_W_down, L0_attn_b_down, L0_attn_W_up,
           L0_attn_b_up, L0_ffn_w_gate_up, L0_ffn_w_down):
    x = np.ascontiguousarray(np.asarray(L0_attn_x, dtype=np.float32))
    gam = np.asarray(L0_attn_ln_gamma, dtype=np.float32)
    bet = np.asarray(L0_attn_ln_beta, dtype=np.float32)
    win = np.asarray(L0_attn_W_in, dtype=np.float32)
    b_in = np.asarray(L0_attn_b_in, dtype=np.float32)
    dlg = np.asarray(L0_attn_decay_logit, dtype=np.float32)
    wout = np.asarray(L0_attn_W_out, dtype=np.float32)
    bout = np.asarray(L0_attn_b_out, dtype=np.float32)
    dsk = np.asarray(L0_attn_D_skip, dtype=np.float32)
    wdn = np.asarray(L0_attn_W_down, dtype=np.float32)
    bdn = np.asarray(L0_attn_b_down, dtype=np.float32)
    wup = np.asarray(L0_attn_W_up, dtype=np.float32)
    bup = np.asarray(L0_attn_b_up, dtype=np.float32)
    wg = np.asarray(L0_ffn_w_gate_up, dtype=np.float32)
    wd = np.asarray(L0_ffn_w_down, dtype=np.float32)

    wg_t = np.ascontiguousarray(
        wg.reshape(128, 128, 16, 128).transpose(0, 3, 2, 1).reshape(128, 128, 2048))
    wd_t = np.ascontiguousarray(
        np.ascontiguousarray(wd.T).reshape(64, 128, 4, 512).transpose(2, 0, 1, 3))
    win_t = np.ascontiguousarray(
        win.reshape(16, 128, 8).transpose(1, 0, 2).reshape(128, 128))
    wdn_t = np.ascontiguousarray(
        wdn.reshape(16, 128, 32).transpose(1, 0, 2).reshape(128, 512))
    wup_aug = np.ascontiguousarray(np.concatenate([wup, bup[None, :]], axis=0))
    gamma_t = np.ascontiguousarray(gam.reshape(16, 128).T)
    beta_t = np.ascontiguousarray(bet.reshape(16, 128).T)
    gamma_rep = np.ascontiguousarray(np.broadcast_to(gam[None, :], (128, DM)))
    dsk_rep = np.ascontiguousarray(np.broadcast_to(dsk[None, :], (128, DM)))
    bout_rep = np.ascontiguousarray(np.broadcast_to(bout[None, :], (128, DM)))
    beta_rep = np.ascontiguousarray(np.broadcast_to(bet[None, :], (128, DM)))
    ident = np.eye(128, dtype=np.float32)

    shared = {
        "wg_t": wg_t, "wd_t": wd_t, "win_t": win_t, "wdn_t": wdn_t,
        "wout": np.ascontiguousarray(wout), "wup_aug": wup_aug,
        "gamma_t": gamma_t, "beta_t": beta_t,
        "gamma_rep": gamma_rep, "dsk_rep": dsk_rep,
        "bout_rep": bout_rep, "beta_rep": beta_rep,
        "b_in": np.ascontiguousarray(b_in.reshape(8, 1)),
        "b_down": np.ascontiguousarray(bdn.reshape(32, 1)),
        "dlogit": np.ascontiguousarray(dlg.reshape(8, 1)),
        "ident": ident,
    }
    in_maps = [dict(shared, x=np.ascontiguousarray(x[b])) for b in range(8)]
    nc = _get_nc()
    res = run_bass_kernel_spmd(nc, in_maps, core_ids=list(range(8)))
    return np.stack([res.results[b]["out"] for b in range(8)], axis=0)
